# version 03
# baseline (speedup 1.0000x reference)
import numpy as np
import concourse.bass as bass
import concourse.mybir as mybir
from concourse.bass_utils import run_bass_kernel_spmd

F32 = mybir.dt.float32
BF16 = mybir.dt.bfloat16
I32 = mybir.dt.int32
AF = mybir.ActivationFunctionType
AL = mybir.AluOpType
AX = mybir.AxisListType.X
MAGIC = float(1.5 * 2 ** 23)

HID = 1024
H = 8
NC8 = 8
FH = 4096
FCH = 32
ENGINES = ("sync", "gpsimd", "scalar", "vector", "tensor")


class MK:
    def __init__(self, nc):
        self.nc = nc
        self.q = {k: [] for k in ENGINES}
        self.cnt = {}
        self.sems = {}
        self.last_wait = {}
        self._sem_stack = []

    def sem(self, name):
        if name not in self.sems:
            cm = self.nc.semaphore(name)
            self.sems[name] = cm.__enter__()
            self._sem_stack.append(cm)
            self.cnt[name] = 0
        return self.sems[name]

    def em(self, eng, fn):
        self.q[eng].append(fn)

    def cut(self, engs=ENGINES):
        for k in engs:
            self.q[k].append(None)

    def wait(self, eng, name, val=None):
        if val is None:
            val = self.cnt[name]
        if val <= 0:
            return
        key = (eng, name)
        if self.last_wait.get(key, -1) >= val:
            return
        self.last_wait[key] = val
        s = self.sem(name)
        self.em(eng, lambda e, s=s, v=val: e.wait_ge(s, v))

    def step(self, eng, name, fn, n=1):
        s = self.sem(name)
        self.cnt[name] += n
        self.em(eng, lambda e, s=s, n=n, fn=fn: fn(e).then_inc(s, n))
        return self.cnt[name]

    def dstep(self, eng, name, fn, n=1):
        self.em(eng, lambda e, fn=fn: fn(e))
        return self.step(eng, name, lambda e: e.drain(), n=n)

    def fstep(self, eng, name, fn, n=1):
        self.cut((eng,))
        self.em(eng, lambda e, fn=fn: fn(e))
        return self.step(eng, name, lambda e: e.drain(), n=n)

    def barrier(self, engs=("sync", "gpsimd", "scalar", "vector", "tensor")):
        for e in engs:
            for s in ("t", "v", "c", "st", "ld", "ldg", "gp"):
                if s in self.cnt:
                    self.wait(e, s)

    def finalize(self, block):
        for eng_name in ENGINES:
            sections = [[]]
            for f in self.q[eng_name]:
                if f is None:
                    if sections[-1]:
                        sections.append([])
                else:
                    sections[-1].append(f)
            for sec in sections:
                if sec:
                    getattr(block, eng_name)(lambda e, sec=sec: [f(e) for f in sec])


def build_program(n_pre=95, n_main=34, debug=(), stop_after=None, skip=()):
    debug = set(debug)
    nc = bass.Bass("TRN2", target_bir_lowering=False)
    mk = MK(nc)
    n_out = n_main - 2
    n_f1 = n_main - 1
    assert n_out % 2 == 0
    n_grp = n_out // 2
    n_pre_tok = max(n_pre * 128, 128)
    n_main_tok = n_main * 128
    n_out_tok = n_out * 128

    xp = nc.dram_tensor("xp", [n_pre_tok, HID], F32, kind="ExternalInput").ap()
    xm = nc.dram_tensor("xm", [n_main_tok, HID], F32, kind="ExternalInput").ap()
    wqT = nc.dram_tensor("wqT", [HID, HID], F32, kind="ExternalInput").ap()
    wkT = nc.dram_tensor("wkT", [HID, HID], F32, kind="ExternalInput").ap()
    wvT = nc.dram_tensor("wvT", [HID, HID], F32, kind="ExternalInput").ap()
    woT = nc.dram_tensor("woT", [HID, HID], F32, kind="ExternalInput").ap()
    normw = nc.dram_tensor("normw", [128, 1], F32, kind="ExternalInput").ap()
    w1T = nc.dram_tensor("w1T", [HID, FH], F32, kind="ExternalInput").ap()
    w2T = nc.dram_tensor("w2T", [FH, HID], F32, kind="ExternalInput").ap()
    convw = nc.dram_tensor("convw", [128, 96], F32, kind="ExternalInput").ap()
    convb = nc.dram_tensor("convb", [128, 32], F32, kind="ExternalInput").ap()
    hmask = nc.dram_tensor("hmask", [128, 2], F32, kind="ExternalInput").ap()
    out_d = nc.dram_tensor("out", [n_out_tok, HID], F32, kind="ExternalOutput").ap()
    xln_s = nc.dram_tensor("xln_s", [n_pre_tok, 1025], BF16, kind="Internal").ap()
    xmid_s = nc.dram_tensor("xmid_s", [n_main_tok, HID], F32, kind="Internal").ap()
    w1q_s = nc.dram_tensor("w1q_s", [HID, FH], BF16, kind="Internal").ap()
    w2q_s = nc.dram_tensor("w2q_s", [FH, HID], BF16, kind="Internal").ap()
    grow_s = nc.dram_tensor("grow_s", [1024], BF16, kind="Internal").ap()
    flush_s = nc.dram_tensor("flush_s", [128, 32], F32, kind="Internal").ap()
    from contextlib import ExitStack
    with ExitStack() as _es:
        def _sb(*a):
            return _es.enter_context(nc.sbuf_tensor(*a))
        def _ps(*a):
            return _es.enter_context(nc.psum_tensor(*a))
        wA_t = _sb("wA", [128, 32768], BF16)
        wB_t = _sb("wB", [128, 32768], BF16)
        ident_t = _sb("ident", [128, 128], BF16)
        mask_t = _sb("mask_ut", [128, 128], F32)
        onesb_t = _sb("ones_bf", [128, 128], BF16)
        onesf_t = _sb("ones_f", [128, 128], F32)
        cvec_t = _sb("cvec", [128, 8], F32)
        zpool_t = _sb("zpool", [128, 32], F32)
        normw_t = _sb("normw_sb", [128, 1], F32)
        cw_t = _sb("cw_sb", [128, 96], F32)
        cb_t = _sb("cb_sb", [128, 32], F32)
        hm_t = _sb("hmask_sb", [128, 2], F32)
        wscale_t = _sb("wscale", [128, 4], F32)
        statef_t = _sb("state_f", [128, 1032], F32)
        stateb_t = _sb("state_b", [128, 1032], BF16)
        stats_t = _sb("stats", [128, 120], F32)
        arena_t = _sb("arena", [128, 18192], F32)
        fb0 = _ps("fb0", [128, 512], F32)
        fb1 = _ps("fb1", [128, 512], F32)
        fb2 = _ps("fb2", [128, 512], F32)
        fb3 = _ps("fb3", [128, 512], F32)
        fb4 = _ps("fb4", [128, 512], F32)
        fb5 = _ps("fb5", [128, 512], F32)
        fb6 = _ps("fb6", [128, 512], F32)
        fb7 = _ps("fb7", [128, 512], F32)
        block = _es.enter_context(nc.Block())
        wA = wA_t.ap()
        wB = wB_t.ap()
        ident = ident_t.ap()
        mask_ut = mask_t.ap()
        ones_bf = onesb_t.ap()
        ones_f = onesf_t.ap()
        cvec = cvec_t.ap()
        zpool = zpool_t.ap()
        normw_sb = normw_t.ap()
        cw_sb = cw_t.ap()
        cb_sb = cb_t.ap()
        hmask_sb = hm_t.ap()
        wscale = wscale_t.ap()
        state_f = statef_t.ap()
        state_b = stateb_t.ap()
        st = stats_t.ap()
        arena = arena_t.ap()
        fb = [fb0, fb1, fb2, fb3, fb4, fb5, fb6, fb7]
        bf = [t.ap().bitcast(BF16) for t in fb]

        class Carve:
            def __init__(self):
                self.cur = 0
            def f32(self, n):
                a = self.cur; self.cur += n
                assert self.cur <= 18192, f"arena overflow {self.cur}"
                return arena[:, a:a + n]
            def bf16(self, n):
                assert n % 2 == 0
                a = self.cur; self.cur += n // 2
                assert self.cur <= 18192, f"arena overflow {self.cur}"
                return arena[:, a:a + n // 2].bitcast(BF16)

        ca = Carve()
        x_sb = [ca.f32(1024), ca.f32(1024)]
        xlna = [ca.bf16(1026), ca.bf16(1026)]
        xln_fm = ca.bf16(1024)
        q_tm = ca.bf16(1024)
        k_tm = ca.bf16(1024)
        v_sb = ca.bf16(1032)
        q_fm = ca.bf16(1024)
        k_fm = ca.bf16(1024)
        scm = ca.bf16(1024)
        o_tm = ca.bf16(1024)
        o_fm = ca.bf16(1024)
        at_bf = ca.bf16(1024)
        sq_scr = ca.f32(1024)
        xmid_sb = [ca.f32(1024), ca.f32(1024)]
        sx_f = ca.f32(8)
        sx_b = ca.bf16(64)
        wtmp = ca.f32(4096)
        wq_raw = ca.f32(2048)
        wq_b = wq_raw.bitcast(BF16)
        pa_end = ca.cur

        cbv = Carve()
        g_buf = cbv.bf16(FCH * 384)
        ys_buf = cbv.bf16(FCH * 256)
        ht_raw = cbv.f32(2048)
        h_tm = ht_raw.bitcast(BF16)
        sqB = ht_raw[:, 0:1024]
        xm2 = [cbv.f32(1024), cbv.f32(1024)]
        xr_sb = xm2
        xln3 = cbv.f32(1024)
        xq_raw = cbv.f32(1024)
        xq_tm = xq_raw[:, 0:512].bitcast(BF16)
        xq_fm = xq_raw[:, 512:1024].bitcast(BF16)
        outb = [xq_raw, xln3]
        xq2 = cbv.bf16(256)
        y_tmp = cbv.bf16(256)
        mxmn_raw = cbv.f32(256)
        mx_b = mxmn_raw[:, 0:128].bitcast(BF16)
        mn_b = mxmn_raw[:, 128:256].bitcast(BF16)
        vq_scr = mxmn_raw
        s2_bf = cbv.bf16(2)
        s2b = cbv.bf16(256)
        f2_sb = cbv.f32(1024)
        pb_end = cbv.cur
        assert max(pa_end, pb_end) <= 18192, (pa_end, pb_end)

        C_EPS, C_INV1024, C_MAGIC, C_ONE, C_EPS2 = 0, 1, 2, 3, 4
        for s_name in ("ld", "ldg", "st", "t", "v", "c", "gp"):
            mk.sem(s_name)

        def DBG(name, shape, dtype, src_ap, light=False):
            if name not in debug:
                return
            t = nc.dram_tensor("dbg_" + name, list(shape), dtype,
                               kind="ExternalOutput").ap()
            if light:
                for s_name in ("v", "c", "t"):
                    mk.wait("scalar", s_name)
            else:
                mk.barrier(("scalar",))
            mk.step("scalar", "st", lambda e, t=t, a=src_ap: e.dma_start(out=t, in_=a), n=16)

        def init_gp(e):
            e.memset(ident, 0.0)
            e.affine_select(out=ident, in_=ident, compare_op=AL.not_equal,
                            fill=1.0, base=0, pattern=[[-1, 128]], channel_multiplier=1)
            e.memset(mask_ut, 0.0)
            e.affine_select(out=mask_ut, in_=mask_ut, compare_op=AL.is_gt,
                            fill=1.0, base=0, pattern=[[-1, 128]], channel_multiplier=1)
            e.memset(ones_bf, 1.0)
            e.memset(ones_f, 1.0)
            e.memset(cvec[:, C_EPS:C_EPS + 1], 1e-5)
            e.memset(cvec[:, C_INV1024:C_INV1024 + 1], 1.0 / 1024.0)
            e.memset(cvec[:, C_MAGIC:C_MAGIC + 1], MAGIC)
            e.memset(cvec[:, C_ONE:C_ONE + 1], 1.0)
            e.memset(zpool, 0.0)
            return e.memset(cvec[:, C_EPS2:C_EPS2 + 1], 1e-5)
        mk.step("gpsimd", "gp", init_gp)

        def init_v(e):
            if "vones" not in skip:
                for h in range(H):
                    e.memset(v_sb[:, 129 * h + 128:129 * h + 129], 1.0)
            return e.memset(st, 0.0)
        mk.step("vector", "v", init_v)

        if "casts" not in skip:
            for i, w in enumerate((wqT, wkT, wvT, woT)):
                mk.step("gpsimd", "ldg", lambda e, i=i, w=w: e.dma_start(
                    out=wA[:, 8192 * i:8192 * (i + 1)].rearrange("p (c e) -> p c e", e=1024),
                    in_=w.rearrange("(c p) e -> p c e", p=128)), n=16)
        attw_ldg = mk.cnt["ldg"]
        small_pending = True


        def wq_pass1(wdram, n_tiles, tile_cols, scale_col):
            csz = min(2048, tile_cols)
            wtA = wtmp[:, 0:csz]
            wtB = wtmp[:, 2048:2048 + csz]
            n_rounds = max(1, tile_cols // 2048)
            nparts = n_tiles * (tile_cols // 1024)
            assert nparts <= 32
            part = 0
            for i in range(n_tiles):
                for rnd in range(n_rounds):
                    mk.cut()
                    mk.wait("sync", "v")
                    lo = 2048 * rnd
                    mk.step("sync", "ld", lambda e, i=i, lo=lo, csz=csz, wdram=wdram: e.dma_start(
                        out=wtA, in_=wdram[128 * i:128 * (i + 1), lo:lo + csz]), n=16)
                    mk.step("sync", "ld", lambda e, i=i, lo=lo, csz=csz, wdram=wdram: e.dma_start(
                        out=wtB, in_=wdram[128 * i:128 * (i + 1), lo:lo + csz]), n=16)
                    mk.wait("vector", "ld")
                    for sub in range(csz // 1024):
                        colA = 48 + part
                        colB = 88 + part
                        mk.fstep("vector", "v", lambda e, colA=colA, sub=sub:
                                e.tensor_reduce(out=st[:, colA:colA + 1],
                                                in_=wtA[:, 1024 * sub:1024 * (sub + 1)],
                                                axis=AX, op=AL.add, apply_absolute_value=True))
                        mk.fstep("vector", "v", lambda e, colB=colB, sub=sub:
                                e.tensor_reduce(out=st[:, colB:colB + 1],
                                                in_=wtB[:, 1024 * sub:1024 * (sub + 1)],
                                                axis=AX, op=AL.add, apply_absolute_value=True))
                        part += 1
                    mk.step("vector", "v", lambda e: e.memset(wtA, 0.0))
                    mk.step("vector", "v", lambda e: e.memset(wtB, 0.0))
            mk.dstep("vector", "v", lambda e, nparts=nparts: e.tensor_tensor(
                out=st[:, 48:48 + nparts], in0=st[:, 48:48 + nparts],
                in1=st[:, 88:88 + nparts], op=AL.max))
            mk.wait("tensor", "v")
            mk.step("tensor", "t", lambda e, nparts=nparts: e.matmul(
                fb0[:, 0:nparts], ones_f, st[:, 48:48 + nparts], start=True, stop=True))
            mk.wait("vector", "t")
            nelem = float(n_tiles * 128 * tile_cols)
            mk.fstep("vector", "v", lambda e, nparts=nparts: e.tensor_reduce(
                out=st[:, 30:31], in_=fb0[:, 0:nparts], axis=AX, op=AL.add))
            mk.fstep("vector", "v", lambda e, nelem=nelem: e.tensor_scalar(
                out=st[:, 31:32], in0=st[:, 30:31], scalar1=1.0 / nelem, scalar2=1e-5,
                op0=AL.mult, op1=AL.max))
            mk.dstep("vector", "v", lambda e, sc=scale_col: e.tensor_copy(
                wscale[:, sc + 1:sc + 2], st[:, 31:32]))
            mk.fstep("vector", "v", lambda e, sc=scale_col: e.reciprocal(
                wscale[:, sc:sc + 1], st[:, 31:32]))

        mk.step("vector", "v", lambda e: e.memset(wtmp, 0.0))
        wq_pass1(w1T, NC8, 4096, 0)
        wq_pass1(w2T, FCH, 1024, 2)
        mk.step("sync", "ld", lambda e: e.dma_start(out=normw_sb, in_=normw), n=16)
        mk.step("sync", "ld", lambda e: e.dma_start(out=cw_sb, in_=convw), n=16)
        mk.step("sync", "ld", lambda e: e.dma_start(out=cb_sb, in_=convb), n=16)
        mk.step("sync", "ld", lambda e: e.dma_start(out=hmask_sb, in_=hmask), n=16)
        small_ld = mk.cnt["ld"]
        mk.wait("vector", "ldg", attw_ldg)
        mk.wait("vector", "ld", small_ld)
        mk.step("vector", "v", lambda e: e.tensor_scalar(
            out=wA[:, 24576:32768], in0=wA[:, 24576:32768],
            scalar1=normw_sb[:, 0:1], scalar2=None, op0=AL.mult))
        DBG("wq1st", (128, 80), F32, st[:, 0:80])
        DBG("wq1wt", (128, 4096), F32, wtmp)
        wq_scale_v = mk.cnt["v"]

        if stop_after == "wq":
            DBG("wsc", (128, 4), F32, wscale)
            mk.wait("scalar", "st")
            mk.finalize(block)
            return nc
        def emit_ln(srcs, sq_buf, eps_col=0):
            off = 0
            for j, ap in enumerate(srcs):
                mk.cut(("vector",))
                mk.fstep("vector", "v", lambda e, ap=ap, j=j: e.tensor_reduce(
                    out=st[:, 8 + j:9 + j], in_=ap, axis=AX, op=AL.add))
            if len(srcs) > 1:
                mk.dstep("vector", "v", lambda e, n=len(srcs): e.tensor_tensor(
                    out=st[:, 0:1], in0=st[:, 8:9], in1=st[:, 9:10], op=AL.add))
            else:
                mk.dstep("vector", "v", lambda e: e.tensor_copy(st[:, 0:1], st[:, 8:9]))
            mk.wait("scalar", "v")
            mk.wait("scalar", "t")
            for j, ap in enumerate(srcs):
                n = ap.shape[-1]
                mk.cut(("scalar",))
                mk.fstep("scalar", "c", lambda e, ap=ap, off=off, n=n, sq=sq_buf:
                        e.activation(sq[:, off:off + n], ap, AF.Square))
                off += n
            mk.cut(("scalar",))
            mk.wait("scalar", "v")
            mk.fstep("scalar", "c", lambda e: e.activation(st[:, 32:33], st[:, 0:1], AF.Square))
            mk.wait("vector", "c")
            mk.cut(("vector",))
            mk.fstep("vector", "v", lambda e, sq=sq_buf: e.tensor_reduce(
                out=st[:, 1:2], in_=sq[:, 0:1024], axis=AX, op=AL.add))
            mk.step("vector", "v", lambda e: e.scalar_tensor_tensor(
                out=st[:, 2:3], in0=st[:, 32:33], scalar=-1.0 / 1024.0, in1=st[:, 1:2],
                op0=AL.mult, op1=AL.add))
            mk.dstep("vector", "v", lambda e: e.tensor_scalar(
                out=st[:, 5:6], in0=st[:, 0:1], scalar1=1.0 / 1024.0, scalar2=None,
                op0=AL.mult))
            mk.wait("scalar", "v")
            mk.cut(("scalar",))
            mk.fstep("scalar", "c", lambda e, ec=eps_col: e.activation(
                st[:, 3:4], st[:, 2:3], AF.Sqrt,
                bias=cvec[:, ec:ec + 1], scale=cvec[:, C_INV1024:C_INV1024 + 1]))
            mk.wait("vector", "c")
            mk.fstep("vector", "v", lambda e: e.reciprocal(st[:, 4:5], st[:, 3:4]))

        def emit_norm(src, dst):
            mk.step("vector", "v", lambda e, src=src, dst=dst: e.tensor_scalar(
                out=dst, in0=src, scalar1=st[:, 5:6], scalar2=st[:, 4:5],
                op0=AL.subtract, op1=AL.mult))

        mk.step("vector", "v", lambda e: e.memset(xlna[0][:, 0:1], 1.0))
        mk.step("vector", "v", lambda e: e.memset(xlna[1][:, 0:1], 1.0))
        mk.wait("tensor", "v", wq_scale_v)
        xl_free_t = [0, 0]
        xl_free_st = [0, 0]
        x_free_v = [0, 0]
        x_free_c = [0, 0]
        for i in range(n_pre):
            mk.cut()
            b = i % 2
            if i >= 2:
                mk.wait("sync", "v", x_free_v[b])
                mk.wait("sync", "c", x_free_c[b])
            mk.step("sync", "ld", lambda e, i=i, b=b: e.dma_start(
                out=x_sb[b], in_=xp[128 * i:128 * (i + 1), :]), n=16)
            mk.wait("vector", "ld")
            if i >= 2:
                mk.wait("vector", "t", xl_free_t[b])
                mk.wait("vector", "st", xl_free_st[b])
            emit_ln([x_sb[b]], sq_scr)
            emit_norm(x_sb[b], xlna[b][:, 1:1025])
            ln_v = mk.cnt["v"]
            x_free_v[b] = ln_v
            x_free_c[b] = mk.cnt["c"]
            mk.wait("tensor", "v", ln_v)
            for dc in range(NC8):
                mk.step("tensor", "t", lambda e, b=b, dc=dc, i=i: e.matmul(
                    fb[dc][:, 0:512], xlna[b][:, 1 + 128 * dc:1 + 128 * (dc + 1)],
                    xlna[b][:, 0:512], start=(i == 0), stop=(i == n_pre - 1)))
            xl_free_t[b] = mk.cnt["t"]
            mk.wait("scalar", "v", ln_v)
            xl_free_st[b] = mk.step("scalar", "st", lambda e, i=i, b=b: e.dma_start(
                out=xln_s[128 * i:128 * (i + 1), :], in_=xlna[b][:, 0:1025]), n=16)
        pass1_t = mk.cnt["t"]
        DBG("st_p1", (128, 80), F32, st[:, 0:80])
        DBG("xln0", (128, 1026), BF16, xlna[0][:, 0:1026])
        assert n_pre >= 2
        if True:
            mk.wait("vector", "t", pass1_t)
            for dc in range(NC8):
                mk.step("vector", "v", lambda e, dc=dc: e.tensor_copy(
                    sx_f[:, dc:dc + 1], fb[dc][:, 0:1]))
            for dc in range(NC8):
                (mk.dstep if dc == NC8 - 1 else mk.step)("vector", "v",
                    lambda e, dc=dc: e.tensor_scalar(
                        out=sx_b[:, 8 * dc:8 * (dc + 1)], in0=ones_f[:, 0:8],
                        scalar1=sx_f[:, dc:dc + 1], scalar2=None, op0=AL.mult))
            mk.wait("scalar", "t", pass1_t)
            for dc in range(NC8):
                mk.step("scalar", "c", lambda e, dc=dc: e.copy(
                    wB[:, 1024 * dc:1024 * dc + 511], fb[dc][:, 1:512]))
            p1_ev_c = mk.cnt["c"]
            p1_ev_v = mk.cnt["v"]

            wq2_jobs = []
            for i in range(NC8):
                wq2_jobs.append((w1T, i, 4096, 0, w1q_s))
            for i in range(FCH):
                wq2_jobs.append((w2T, i, 1024, 2, w2q_s))
            wq2_i = [0]

            def emit_wq2_one():
                if wq2_i[0] >= len(wq2_jobs):
                    return
                wdram, i, tile_cols, sc, wq_dram = wq2_jobs[wq2_i[0]]
                wq2_i[0] += 1
                mk.wait("sync", "v")
                mk.step("sync", "ld", lambda e, wdram=wdram, i=i, tile_cols=tile_cols:
                        e.dma_start(out=wtmp[:, 0:tile_cols],
                                    in_=wdram[128 * i:128 * (i + 1), :]), n=16)
                mk.wait("vector", "ld")
                mk.wait("vector", "st")
                mk.step("vector", "v", lambda e, sc=sc, tile_cols=tile_cols: e.tensor_scalar(
                    out=wtmp[:, 0:tile_cols], in0=wtmp[:, 0:tile_cols],
                    scalar1=wscale[:, sc:sc + 1], scalar2=MAGIC, op0=AL.mult, op1=AL.add))
                mk.step("vector", "v", lambda e, tile_cols=tile_cols: e.tensor_scalar(
                    out=wtmp[:, 0:tile_cols], in0=wtmp[:, 0:tile_cols],
                    scalar1=MAGIC, scalar2=1.0, op0=AL.subtract, op1=AL.min))
                mk.step("vector", "v", lambda e, tile_cols=tile_cols: e.tensor_scalar(
                    out=wq_b[:, 0:tile_cols], in0=wtmp[:, 0:tile_cols],
                    scalar1=-1.0, scalar2=None, op0=AL.max))
                mk.wait("scalar", "v")
                mk.step("scalar", "st", lambda e, wq_dram=wq_dram, i=i, tile_cols=tile_cols:
                        e.dma_start(out=wq_dram[128 * i:128 * (i + 1), :],
                                    in_=wq_b[:, 0:tile_cols]), n=16)

            p2_free_t = [0, 0]
            for i in range(n_pre):
                mk.cut()
                b = i % 2
                if i < 2:
                    mk.wait("sync", "st", xl_free_st[b])
                    mk.wait("sync", "c", p1_ev_c)
                    mk.wait("sync", "v", p1_ev_v)
                else:
                    mk.wait("sync", "t", p2_free_t[b])
                mk.step("sync", "ld", lambda e, i=i, b=b: e.dma_start(
                    out=xlna[b][:, 0:1025], in_=xln_s[128 * i:128 * (i + 1), :]), n=16)
                mk.wait("tensor", "ld")
                mk.wait("tensor", "c", p1_ev_c)
                mk.wait("tensor", "v", p1_ev_v)
                for dc in range(NC8):
                    rlo = 512 if dc < 7 else 513
                    mk.step("tensor", "t", lambda e, b=b, dc=dc, i=i, rlo=rlo: e.matmul(
                        fb[dc][:, 0:512], xlna[b][:, 1 + 128 * dc:1 + 128 * (dc + 1)],
                        xlna[b][:, rlo:rlo + 512], start=(i == 0), stop=(i == n_pre - 1)))
                p2_free_t[b] = mk.cnt["t"]
                if i % 2 == 1:
                    emit_wq2_one()
            while wq2_i[0] < len(wq2_jobs):
                emit_wq2_one()
            pass2_t = mk.cnt["t"]
            mk.wait("scalar", "t", pass2_t)
            for dc in range(NC8):
                glo = 511 if dc < 7 else 512
                mk.step("scalar", "c", lambda e, dc=dc, glo=glo: e.copy(
                    wB[:, 1024 * dc + glo:1024 * dc + glo + 512], fb[dc][:, 0:512]))
            p2_ev_c = mk.cnt["c"]
            mk.wait("gpsimd", "c", p2_ev_c)
            mk.step("gpsimd", "ldg", lambda e: e.dma_start(
                out=grow_s[0:896].rearrange("(r c) -> r c", r=1),
                in_=wB[127:128, 7 * 1024:7 * 1024 + 896]), n=16)
            mk.wait("gpsimd", "ldg")
            mk.step("gpsimd", "ldg", lambda e: e.dma_start(
                out=wB[:, 0:7 * 1024].rearrange("p (c j) -> p c j", j=1024)[:, :, 1023:1024].squeeze(2),
                in_=grow_s[0:896].rearrange("(c p) -> p c", p=128),
                allow_slow_non_contiguous=True), n=16)
            mk.wait("gpsimd", "ldg")
            mk.step("gpsimd", "ldg", lambda e: e.dma_start(
                out=grow_s[896:1024].rearrange("(r c) -> r c", r=1),
                in_=wB[127:128, 3 * 1024 + 896:3 * 1024 + 1024]), n=16)
            mk.wait("gpsimd", "ldg")
            mk.step("gpsimd", "ldg", lambda e: e.dma_start(
                out=wB[:, 7 * 1024 + 511:7 * 1024 + 512],
                in_=grow_s[896:1024].rearrange("(p c) -> p c", c=1)), n=16)
            gfix_ldg = mk.cnt["ldg"]
        DBG("G", (128, 8192), BF16, wB[:, 0:8192])
        DBG("xln1", (128, 1025), BF16, xlna[1][:, 0:1025])
        DBG("sx", (128, 8), F32, sx_f)

        mk.wait("tensor", "ldg", gfix_ldg)
        mk.wait("tensor", "v")
        for h in range(H):
            mk.cut()
            for dpc in range(NC8):
                tgt = fb[dpc // 4]
                off = 128 * (dpc % 4)
                for dc in range(NC8):
                    mk.step("tensor", "t", lambda e, tgt=tgt, off=off, dc=dc, dpc=dpc, h=h:
                            e.matmul(tgt[:, off:off + 128],
                                     wB[:, 1024 * dc + 128 * dpc:1024 * dc + 128 * (dpc + 1)],
                                     wA[:, 8192 + 1024 * dc + 128 * h:
                                         8192 + 1024 * dc + 128 * (h + 1)],
                                     start=(dc == 0), stop=(dc == NC8 - 1)))
            mk.wait("vector", "t")
            mk.step("vector", "v", lambda e: e.tensor_copy(at_bf[:, 0:512], fb0[:, 0:512]))
            mk.step("vector", "v", lambda e: e.tensor_copy(at_bf[:, 512:1024], fb1[:, 0:512]))
            mk.wait("tensor", "v")
            for dpc in range(NC8):
                mk.step("tensor", "t", lambda e, dpc=dpc, h=h: e.matmul(
                    fb2[:, 0:128], at_bf[:, 128 * dpc:128 * (dpc + 1)],
                    wA[:, 16384 + 1024 * dpc + 128 * h:16384 + 1024 * dpc + 128 * (h + 1)],
                    start=(dpc == 0), stop=(dpc == NC8 - 1)))
            for dc in range(NC8):
                mk.step("tensor", "t", lambda e, dc=dc, h=h: e.matmul(
                    fb3[:, 0:8],
                    wA[:, 8192 + 1024 * dc + 128 * h:8192 + 1024 * dc + 128 * (h + 1)],
                    sx_b[:, 8 * dc:8 * (dc + 1)], start=(dc == 0), stop=(dc == NC8 - 1)))
            mk.wait("vector", "t")
            mk.step("vector", "v", lambda e, h=h: e.tensor_copy(
                state_f[:, 129 * h:129 * h + 128], fb2[:, 0:128]))
            mk.step("vector", "v", lambda e, h=h: e.tensor_copy(
                state_f[:, 129 * h + 128:129 * h + 129], fb3[:, 0:1]))
            mk.wait("tensor", "v")
        mk.step("vector", "v", lambda e: e.tensor_copy(state_b, state_f))
        DBG("state0", (128, 1032), F32, state_f)

        mk.wait("tensor", "ldg", attw_ldg)
        mk.wait("tensor", "v")
        o_tgt = [(fb2, 0), (fb2, 129), (fb2, 258), (fb3, 0), (fb3, 129),
                 (fb3, 258), (fb4, 0), (fb4, 129)]
        d_tgt = [(fb5, 0), (fb5, 129), (fb5, 258), (fb6, 0), (fb6, 129),
                 (fb6, 258), (fb7, 0), (fb7, 129)]
        zt = wq_raw
        mk.wait("vector", "st")
        mk.step("vector", "v", lambda e: e.memset(zt, 0.0))
        hmt = [sq_scr, sq_scr]
        x_free_v = [0, 0]
        x_free_c = [0, 0]
        xl_tr_t = [0, 0]
        xmid_st = [0, 0]
        for i in range(n_main):
            mk.cut()
            b = i % 2
            if i >= 2:
                mk.wait("sync", "v", x_free_v[b])
                mk.wait("sync", "c", x_free_c[b])
            mk.step("sync", "ld", lambda e, i=i, b=b: e.dma_start(
                out=x_sb[b], in_=xm[128 * i:128 * (i + 1), :]), n=16)
            mk.wait("vector", "ld")
            emit_ln([x_sb[b]], sq_scr)
            if i >= 2:
                mk.wait("vector", "t", xl_tr_t[b])
                mk.wait("vector", "st", xmid_st[b])
            emit_norm(x_sb[b], xlna[b][:, 1:1025])
            ln1_v = mk.cnt["v"]
            x_free_c[b] = mk.cnt["c"]
            mk.wait("tensor", "v", ln1_v)
            for c in range(NC8):
                mk.step("tensor", "t", lambda e, b=b, c=c: e.transpose(
                    bf[6][:, 128 * c:128 * (c + 1)],
                    xlna[b][:, 1 + 128 * c:1 + 128 * (c + 1)], ident))
            xl_tr_t[b] = mk.cnt["t"]
            mk.wait("scalar", "t")
            mk.step("scalar", "c", lambda e: e.copy(xln_fm, bf[6][:, 0:1024]))
            mk.wait("tensor", "c")
            for dc in range(NC8):
                for wi, tgt in ((0, (fb0, fb1)), (1, (fb2, fb3)), (2, (fb4, fb5))):
                    for hf in range(2):
                        mk.step("tensor", "t", lambda e, dc=dc, wi=wi, tgt=tgt, hf=hf:
                                e.matmul(tgt[hf][:, 0:512],
                                         xln_fm[:, 128 * dc:128 * (dc + 1)],
                                         wA[:, 8192 * wi + 1024 * dc + 512 * hf:
                                             8192 * wi + 1024 * dc + 512 * (hf + 1)],
                                         start=(dc == 0), stop=(dc == NC8 - 1)))
            qkv_t = mk.cnt["t"]
            mk.wait("scalar", "t", qkv_t)
            mk.step("scalar", "c", lambda e: e.copy(q_tm[:, 0:512], fb0[:, 0:512]))
            mk.step("scalar", "c", lambda e: e.copy(q_tm[:, 512:1024], fb1[:, 0:512]))
            mk.step("scalar", "c", lambda e: e.copy(k_tm[:, 0:512], fb2[:, 0:512]))
            mk.step("scalar", "c", lambda e: e.copy(k_tm[:, 512:1024], fb3[:, 0:512]))
            qk_c = mk.cnt["c"]
            mk.wait("vector", "t", qkv_t)
            vdst = v_sb.rearrange("p (g c) -> p g c", c=129)[:, :, 0:128]
            mk.step("vector", "v", lambda e, vdst=vdst: e.tensor_copy(
                vdst[:, 0:4], fb4[:, 0:512].rearrange("p (g c) -> p g c", c=128)))
            mk.step("vector", "v", lambda e, vdst=vdst: e.tensor_copy(
                vdst[:, 4:8], fb5[:, 0:512].rearrange("p (g c) -> p g c", c=128)))
            v_ev = mk.cnt["v"]
            mk.wait("tensor", "c", qk_c)
            for c in range(NC8):
                mk.step("tensor", "t", lambda e, c=c: e.transpose(
                    bf[6][:, 128 * c:128 * (c + 1)], q_tm[:, 128 * c:128 * (c + 1)], ident))
            for c in range(NC8):
                mk.step("tensor", "t", lambda e, c=c: e.transpose(
                    bf[7][:, 128 * c:128 * (c + 1)], k_tm[:, 128 * c:128 * (c + 1)], ident))
            mk.wait("scalar", "t")
            mk.step("scalar", "c", lambda e: e.copy(q_fm, bf[6][:, 0:1024]))
            mk.step("scalar", "c", lambda e: e.copy(k_fm, bf[7][:, 0:1024]))
            qkfm_c = mk.cnt["c"]
            mk.wait("tensor", "c", qkfm_c)
            for h in range(4):
                mk.step("tensor", "t", lambda e, h=h: e.matmul(
                    fb0[:, 128 * h:128 * (h + 1)], k_fm[:, 128 * h:128 * (h + 1)],
                    q_fm[:, 128 * h:128 * (h + 1)], start=True, stop=True))
            scA_t = mk.cnt["t"]
            for h in range(4, 8):
                mk.step("tensor", "t", lambda e, h=h: e.matmul(
                    fb1[:, 128 * (h - 4):128 * (h - 3)], k_fm[:, 128 * h:128 * (h + 1)],
                    q_fm[:, 128 * h:128 * (h + 1)], start=True, stop=True))
            scB_t = mk.cnt["t"]
            mk.wait("vector", "t", scA_t)
            for h in range(4):
                mk.step("vector", "v", lambda e, h=h: e.tensor_tensor(
                    out=scm[:, 128 * h:128 * (h + 1)], in0=fb0[:, 128 * h:128 * (h + 1)],
                    in1=mask_ut, op=AL.mult))
            mk.wait("vector", "t", scB_t)
            for h in range(4, 8):
                mk.step("vector", "v", lambda e, h=h: e.tensor_tensor(
                    out=scm[:, 128 * h:128 * (h + 1)],
                    in0=fb1[:, 128 * (h - 4):128 * (h - 3)], in1=mask_ut, op=AL.mult))
            scm_v = mk.cnt["v"]
            mk.wait("tensor", "v", max(scm_v, v_ev))
            for h in range(H):
                tgt, off = o_tgt[h]
                mk.step("tensor", "t", lambda e, h=h, tgt=tgt, off=off: e.matmul(
                    tgt[:, off:off + 129], scm[:, 128 * h:128 * (h + 1)],
                    v_sb[:, 129 * h:129 * h + 129], start=True, stop=False))
                mk.step("tensor", "t", lambda e, h=h, tgt=tgt, off=off: e.matmul(
                    tgt[:, off:off + 129], q_fm[:, 128 * h:128 * (h + 1)],
                    state_b[:, 129 * h:129 * h + 129], start=False, stop=True))
            o_t = mk.cnt["t"]
            for h in range(H):
                tgt, off = d_tgt[h]
                mk.step("tensor", "t", lambda e, h=h, tgt=tgt, off=off: e.matmul(
                    tgt[:, off:off + 129], k_tm[:, 128 * h:128 * (h + 1)],
                    v_sb[:, 129 * h:129 * h + 129], start=True, stop=True))
            d_t = mk.cnt["t"]
            mk.wait("vector", "t", o_t)
            mk.step("vector", "v", lambda e: e.tensor_copy(
                st[:, 8:11],
                fb2[:, 0:387].rearrange("p (g c) -> p g c", c=129)[:, :, 128]))
            mk.step("vector", "v", lambda e: e.tensor_copy(
                st[:, 11:14],
                fb3[:, 0:387].rearrange("p (g c) -> p g c", c=129)[:, :, 128]))
            mk.dstep("vector", "v", lambda e: e.tensor_copy(
                st[:, 14:16],
                fb4[:, 0:258].rearrange("p (g c) -> p g c", c=129)[:, :, 128]))
            mk.wait("scalar", "t", o_t)
            for h in range(H):
                tgt, off = o_tgt[h]
                mk.cut(("scalar",))
                mk.fstep("scalar", "c", lambda e, tgt=tgt, off=off, h=h: e.activation(
                    sq_scr[:, 128 * h:128 * (h + 1)], tgt[:, off:off + 128], AF.Square))
            mk.wait("vector", "c")
            mk.cut(("vector",))
            mk.fstep("vector", "v", lambda e: e.tensor_reduce(
                out=st[:, 16:24],
                in_=sq_scr[:, 0:1024].rearrange("p (h c) -> p h c", c=128),
                axis=AX, op=AL.add))
            mk.wait("scalar", "v")
            mk.cut(("scalar",))
            mk.fstep("scalar", "c", lambda e: e.activation(st[:, 40:48], st[:, 8:16], AF.Square))
            mk.wait("vector", "c")
            mk.dstep("vector", "v", lambda e: e.tensor_scalar(
                out=st[:, 8:16], in0=st[:, 40:48], scalar1=1.0, scalar2=1e-5,
                op0=AL.max, op1=AL.mult))
            mk.dstep("vector", "v", lambda e: e.scalar_tensor_tensor(
                out=st[:, 8:16], in0=st[:, 16:24], scalar=1.0 / 128.0, in1=st[:, 8:16],
                op0=AL.mult, op1=AL.add))
            mk.wait("scalar", "v")
            mk.cut(("scalar",))
            mk.fstep("scalar", "c", lambda e: e.activation(st[:, 8:16], st[:, 8:16], AF.Sqrt))
            mk.wait("vector", "c")
            mk.fstep("vector", "v", lambda e: e.reciprocal(st[:, 8:16], st[:, 8:16]))
            for h in range(H):
                tgt, off = o_tgt[h]
                mk.step("vector", "v", lambda e, h=h, tgt=tgt, off=off: e.tensor_scalar(
                    out=o_tm[:, 128 * h:128 * (h + 1)], in0=tgt[:, off:off + 128],
                    scalar1=st[:, 8 + h:9 + h], scalar2=None, op0=AL.mult))
            oap_v = mk.cnt["v"]
            mk.wait("vector", "t", d_t)
            mk.step("vector", "v", lambda e: e.tensor_tensor(
                out=state_f[:, 0:387], in0=state_f[:, 0:387], in1=fb5[:, 0:387], op=AL.add))
            mk.step("vector", "v", lambda e: e.tensor_tensor(
                out=state_f[:, 387:774], in0=state_f[:, 387:774], in1=fb6[:, 0:387], op=AL.add))
            mk.step("vector", "v", lambda e: e.tensor_tensor(
                out=state_f[:, 774:1032], in0=state_f[:, 774:1032], in1=fb7[:, 0:258], op=AL.add))
            mk.step("vector", "v", lambda e: e.tensor_copy(state_b, state_f))
            mk.wait("tensor", "v", oap_v)
            for c in range(NC8):
                mk.step("tensor", "t", lambda e, c=c: e.transpose(
                    bf[0][:, 128 * c:128 * (c + 1)], o_tm[:, 128 * c:128 * (c + 1)], ident))
            mk.wait("scalar", "t")
            mk.step("scalar", "c", lambda e: e.copy(o_fm, bf[0][:, 0:1024]))
            ofm_c = mk.cnt["c"]
            mk.wait("tensor", "c", ofm_c)
            mk.wait("tensor", "v", oap_v)
            for ec in range(NC8):
                for hf in range(2):
                    mk.step("tensor", "t", lambda e, ec=ec, hf=hf: e.matmul(
                        (fb2 if hf == 0 else fb3)[:, 0:512],
                        o_fm[:, 128 * ec:128 * (ec + 1)],
                        wA[:, 24576 + 1024 * ec + 512 * hf:
                            24576 + 1024 * ec + 512 * (hf + 1)],
                        start=(ec == 0), stop=(ec == NC8 - 1)))
            op_t = mk.cnt["t"]
            mk.wait("vector", "t", op_t)
            mk.wait("scalar", "t", op_t)
            emit_ln([fb2[:, 0:512], fb3[:, 0:512]], sq_scr, eps_col=C_EPS2)
            for hf, bank in ((0, fb2), (1, fb3)):
                emit_norm(bank[:, 0:512], sq_scr[:, 0:512])
                mk.step("vector", "v", lambda e, b=b, hf=hf: e.tensor_tensor(
                    out=xmid_sb[b][:, 512 * hf:512 * (hf + 1)], in0=sq_scr[:, 0:512],
                    in1=x_sb[b][:, 512 * hf:512 * (hf + 1)], op=AL.add))
            xmid_v = mk.cnt["v"]
            if i == 1:
                DBG("t2_xln", (128, 1026), BF16, xlna[b][:, 0:1026], light=True)
                DBG("t2_q", (128, 1024), BF16, q_tm, light=True)
                DBG("t2_k", (128, 1024), BF16, k_tm, light=True)
                DBG("t2_v", (128, 1032), BF16, v_sb, light=True)
                DBG("t2_scm", (128, 1024), BF16, scm, light=True)
                DBG("t2_otm", (128, 1024), BF16, o_tm, light=True)
                DBG("t2_state", (128, 1032), F32, state_f, light=True)
                DBG("t2_xmid", (128, 1024), F32, xmid_sb[b], light=True)
                DBG("t2_st", (128, 80), F32, st[:, 0:80], light=True)
            x_free_v[b] = xmid_v
            mk.wait("scalar", "v", xmid_v)
            xmid_st[b] = mk.step("scalar", "st", lambda e, i=i, b=b: e.dma_start(
                out=xmid_s[128 * i:128 * (i + 1), :], in_=xmid_sb[b]), n=16)
        DBG("xmid_full", (n_main_tok, HID), F32, xmid_s)
        mk.barrier()
        mk.step("sync", "ld", lambda e: e.dma_start(
            out=wA.rearrange("p (c h) -> p c h", h=4096),
            in_=w1q_s.rearrange("(c p) h -> p c h", p=128)), n=16)
        mk.step("sync", "ld", lambda e: e.dma_start(
            out=wB.rearrange("p (c d) -> p c d", d=1024),
            in_=w2q_s.rearrange("(c p) d -> p c d", p=128)), n=16)
        wq_ld = mk.cnt["ld"]
        mk.wait("scalar", "ld", wq_ld)
        DBG("wsc", (128, 4), F32, wscale)
        DBG("w1qA", (128, 4096), BF16, wA[:, 0:4096])
        DBG("w2qB", (128, 1024), BF16, wB[:, 0:1024])
        mk.wait("tensor", "ld", wq_ld)

        def slot_of(j):
            return 0 if j == 0 else (1 if j % 2 == 1 else 2)

        def emit_ffn1_tile(j):
            mk.cut()
            b = j % 2
            mk.wait("sync", "v")
            mk.step("sync", "ld", lambda e, j=j, b=b: e.dma_start(
                out=xm2[b], in_=xmid_s[127 + 128 * j:255 + 128 * j, :]), n=16)
            mk.wait("vector", "ld")
            emit_ln([xm2[b]], sqB)
            emit_norm(xm2[b], xln3)
            mk.cut(("vector",))
            mk.fstep("vector", "v", lambda e: e.tensor_reduce(
                out=st[:, 6:7], in_=xln3, axis=AX, op=AL.max, apply_absolute_value=True))
            mk.step("vector", "v", lambda e: e.tensor_single_scalar(
                out=st[:, 6:7], in_=st[:, 6:7], scalar=1e-5, op=AL.max))
            mk.fstep("vector", "v", lambda e: e.reciprocal(st[:, 7:8], st[:, 6:7]))
            mk.step("vector", "v", lambda e: e.tensor_single_scalar(
                out=st[:, 7:8], in_=st[:, 7:8], scalar=127.0, op=AL.mult))
            mk.dstep("vector", "v", lambda e: e.tensor_single_scalar(
                out=st[:, 6:7], in_=st[:, 6:7], scalar=1.0 / 127.0, op=AL.mult))
            mk.step("vector", "v", lambda e: e.tensor_scalar(
                out=xln3, in0=xln3, scalar1=st[:, 7:8], scalar2=MAGIC,
                op0=AL.mult, op1=AL.add))
            mk.step("vector", "v", lambda e: e.tensor_scalar(
                out=xq_tm, in0=xln3, scalar1=MAGIC, scalar2=st[:, 6:7],
                op0=AL.subtract, op1=AL.mult))
            q1_v = mk.cnt["v"]
            mk.wait("tensor", "v", q1_v)
            for c in range(NC8):
                mk.step("tensor", "t", lambda e, c=c: e.transpose(
                    bf[0][:, 128 * c:128 * (c + 1)], xq_tm[:, 128 * c:128 * (c + 1)], ident))
            mk.wait("scalar", "t")
            mk.step("scalar", "c", lambda e: e.copy(xq_fm, bf[0][:, 0:1024]))
            mk.wait("tensor", "c")
            for dc in range(NC8):
                for jj in range(8):
                    mk.step("tensor", "t", lambda e, dc=dc, jj=jj: e.matmul(
                        fb[jj][:, 0:512], xq_fm[:, 128 * dc:128 * (dc + 1)],
                        wA[:, 4096 * dc + 512 * jj:4096 * dc + 512 * (jj + 1)],
                        start=(dc == 0), stop=(dc == NC8 - 1)))
            f1_t = mk.cnt["t"]
            mk.wait("vector", "t", f1_t)
            for jj in range(8):
                mk.step("vector", "v", lambda e, jj=jj: e.tensor_scalar(
                    out=h_tm[:, 512 * jj:512 * (jj + 1)], in0=fb[jj][:, 0:512],
                    scalar1=wscale[:, 1:2], scalar2=None, op0=AL.mult))
            hev_v = mk.cnt["v"]
            s = slot_of(j)
            mk.wait("tensor", "v", hev_v)
            gview = g_buf.rearrange("p (c w) -> p c w", w=384)
            for r in range(4):
                bk = bf[r % 2]
                for cc in range(8):
                    c = 8 * r + cc
                    mk.step("tensor", "t", lambda e, bk=bk, cc=cc, c=c: e.transpose(
                        bk[:, 128 * cc:128 * (cc + 1)], h_tm[:, 128 * c:128 * (c + 1)],
                        ident))
                mk.wait("vector", "t")
                mk.step("vector", "v", lambda e, bk=bk, r=r, s=s, gview=gview: e.tensor_copy(
                    gview[:, 8 * r:8 * r + 8, 128 * s:128 * (s + 1)],
                    bk[:, 0:1024].rearrange("p (c w) -> p c w", w=128)))
                mk.wait("tensor", "v")

        def emit_group(gi):
            mk.cut()
            for k in range(2):
                mk.wait("sync", "v")
                mk.step("sync", "ld", lambda e, gi=gi, k=k: e.dma_start(
                    out=xr_sb[k], in_=xmid_s[128 + 256 * gi + 128 * k:
                                             128 + 256 * gi + 128 * (k + 1), :]), n=16)
            xr_ld = mk.cnt["ld"]
            for c in range(FCH):
                gw = g_buf[:, 384 * c:384 * (c + 1)]
                ys = ys_buf[:, 256 * c:256 * (c + 1)]
                if c > 0:
                    mk.wait("vector", "c")
                mk.step("vector", "v", lambda e, gw=gw, c=c: e.tensor_scalar(
                    out=y_tmp, in0=gw[:, 0:256], scalar1=cw_sb[:, 3 * c:3 * c + 1],
                    scalar2=cb_sb[:, c:c + 1], op0=AL.mult, op1=AL.add))
                mk.step("vector", "v", lambda e, gw=gw, c=c: e.scalar_tensor_tensor(
                    out=y_tmp, in0=gw[:, 1:257], scalar=cw_sb[:, 3 * c + 1:3 * c + 2],
                    in1=y_tmp, op0=AL.mult, op1=AL.add))
                mk.step("vector", "v", lambda e, gw=gw, c=c: e.scalar_tensor_tensor(
                    out=y_tmp, in0=gw[:, 2:258], scalar=cw_sb[:, 3 * c + 2:3 * c + 3],
                    in1=y_tmp, op0=AL.mult, op1=AL.add))
                yv = mk.cnt["v"]
                mk.wait("scalar", "v", yv)
                mk.cut(("scalar",))
                mk.fstep("scalar", "c", lambda e, ys=ys, c=c: e.activation(
                    ys, y_tmp, AF.Silu, bias=zpool[:, c:c + 1]))
                if c == 0:
                    mk.step("vector", "v", lambda e: e.tensor_copy(mn_b, y_tmp))
                    mk.step("vector", "v", lambda e: e.tensor_copy(mx_b, y_tmp))
                else:
                    mk.step("vector", "v", lambda e: e.tensor_tensor(
                        out=mn_b, in0=mn_b, in1=y_tmp, op=AL.min))
                    mk.step("vector", "v", lambda e: e.tensor_tensor(
                        out=mx_b, in0=mx_b, in1=y_tmp, op=AL.max))
            mm_v = mk.cnt["v"]
            if first_group_dump[0]:
                DBG("g0", (128, 384), BF16, g_buf[:, 0:384], light=True)
                DBG("ys0", (128, 256), BF16, ys_buf[:, 0:256], light=True)
            mk.cut()
            gview = g_buf.rearrange("p (c w) -> p c w", w=384)
            mk.step("vector", "v", lambda e, gview=gview: e.tensor_copy(
                gview[:, :, 0:128], gview[:, :, 256:384]))
            mk.wait("scalar", "v", mm_v)
            mk.cut(("scalar",))
            mk.fstep("scalar", "c", lambda e: e.activation(mx_b, mx_b, AF.Silu))
            mk.cut(("scalar",))
            mk.fstep("scalar", "c", lambda e: e.activation(mn_b, mn_b, AF.Silu))
            mk.wait("vector", "c")
            mk.step("vector", "v", lambda e: e.tensor_scalar(
                out=mn_b, in0=mn_b, scalar1=-1.0, scalar2=None, op0=AL.mult))
            mk.step("vector", "v", lambda e: e.tensor_tensor(
                out=mx_b, in0=mx_b, in1=mn_b, op=AL.max))
            mab_v = mk.cnt["v"]
            mk.wait("tensor", "v", mab_v)
            for k in range(2):
                mk.step("tensor", "t", lambda e, k=k: e.transpose(
                    bf[2][:, 128 * k:128 * (k + 1)], mx_b[:, 128 * k:128 * (k + 1)], ident))
            mk.wait("vector", "t")
            for k in range(2):
                mk.cut(("vector",))
                mk.fstep("vector", "v", lambda e, k=k: e.tensor_reduce(
                    out=st[:, 6 + k:7 + k], in_=bf[2][:, 128 * k:128 * (k + 1)],
                    axis=AX, op=AL.max, apply_absolute_value=True))
            mk.step("vector", "v", lambda e: e.tensor_scalar(
                out=st[:, 6:8], in0=st[:, 6:8], scalar1=1e-5, scalar2=None, op0=AL.max))
            mk.fstep("vector", "v", lambda e: e.reciprocal(st[:, 8:10], st[:, 6:8]))
            mk.dstep("vector", "v", lambda e: e.tensor_single_scalar(
                out=st[:, 8:10], in_=st[:, 8:10], scalar=127.0, op=AL.mult))
            mk.dstep("vector", "v", lambda e: e.tensor_copy(s2_bf, st[:, 8:10]))
            mk.dstep("vector", "v", lambda e: e.tensor_copy(st[:, 8:10], s2_bf))
            mk.fstep("vector", "v", lambda e: e.reciprocal(st[:, 10:12], st[:, 8:10]))
            mk.dstep("vector", "v", lambda e: e.tensor_scalar(
                out=st[:, 10:12], in0=st[:, 10:12], scalar1=wscale[:, 3:4], scalar2=None,
                op0=AL.mult))
            s2_v = mk.cnt["v"]
            mk.wait("tensor", "v", s2_v)
            for k in range(2):
                mk.step("tensor", "t", lambda e, k=k: e.transpose(
                    bf[3][:, 128 * k:128 * (k + 1)],
                    s2_bf[:, k:k + 1].to_broadcast([128, 128]), ident))
            mk.wait("vector", "t")
            mk.step("vector", "v", lambda e: e.tensor_copy(s2b, bf[3][:, 0:256]))
            s2b_v = mk.cnt["v"]
            if first_group_dump[0]:
                DBG("s2g0", (128, 16), F32, st[:, 0:16], light=True)
                DBG("s2b0", (128, 256), BF16, s2b, light=True)
            mk.wait("tensor", "v", s2b_v)
            for c in range(FCH):
                ys = ys_buf[:, 256 * c:256 * (c + 1)]
                mk.wait("vector", "t")
                mk.step("vector", "v", lambda e, ys=ys: e.tensor_tensor(
                    out=vq_scr, in0=ys, in1=s2b, op=AL.mult))
                mk.step("vector", "v", lambda e: e.tensor_single_scalar(
                    out=vq_scr, in_=vq_scr, scalar=MAGIC, op=AL.add))
                mk.dstep("vector", "v", lambda e: e.tensor_scalar(
                    out=xq2, in0=vq_scr, scalar1=MAGIC, scalar2=None, op0=AL.subtract))
                xq2_v = mk.cnt["v"]
                if first_group_dump[0] and c == 0:
                    DBG("vq0", (128, 256), F32, vq_scr)
                    DBG("xq20", (128, 256), BF16, xq2)
                    DBG("stq0", (128, 80), F32, st[:, 0:80])
                mk.wait("tensor", "v", xq2_v)
                for k in range(2):
                    for hf in range(2):
                        tb = fb[4 + 2 * k + hf]
                        mk.step("tensor", "t", lambda e, c=c, k=k, hf=hf, tb=tb: e.matmul(
                            tb[:, 0:512], xq2[:, 128 * k:128 * (k + 1)],
                            wB[:, 1024 * c + 512 * hf:1024 * c + 512 * (hf + 1)],
                            start=(c == 0), stop=(c == FCH - 1)))
            f2_t = mk.cnt["t"]
            mk.wait("vector", "t", f2_t)
            mk.wait("vector", "ld", xr_ld)
            for k in range(2):
                mk.cut()
                for hf in range(2):
                    tb = fb[4 + 2 * k + hf]
                    mk.step("vector", "v", lambda e, tb=tb, hf=hf, k=k: e.tensor_scalar(
                        out=f2_sb[:, 512 * hf:512 * (hf + 1)], in0=tb[:, 0:512],
                        scalar1=st[:, 10 + k:11 + k], scalar2=None, op0=AL.mult))
                emit_ln([f2_sb], sqB, eps_col=C_EPS2)
                emit_norm(f2_sb, sqB)
                mk.step("vector", "v", lambda e, k=k: e.tensor_tensor(
                    out=outb[k], in0=sqB, in1=xr_sb[k], op=AL.add))
                ob_v = mk.cnt["v"]
                if first_group_dump[0] and k == 0:
                    DBG("f20", (128, 1024), F32, f2_sb, light=True)
                    DBG("ob0", (128, 1024), F32, outb[0], light=True)
                mk.wait("scalar", "v", ob_v)
                mk.step("scalar", "st", lambda e, gi=gi, k=k: e.dma_start(
                    out=out_d[256 * gi + 128 * k:256 * gi + 128 * (k + 1), :],
                    in_=outb[k]), n=16)

        first_group_dump = [True]
        for gi in range(n_grp):
            if gi == 0:
                for j in range(3):
                    emit_ffn1_tile(j)
            else:
                for j in (2 * gi + 1, 2 * gi + 2):
                    emit_ffn1_tile(j)
            emit_group(gi)
            first_group_dump[0] = False

        mk.wait("scalar", "st")
        mk.finalize(block)
    return nc


S = 16384
B = 2
NPRE = 95
NMAIN = 34


def _in_maps(inputs):
    x = np.ascontiguousarray(inputs["L0_attn_hidden_states"], dtype=np.float32)
    wq = np.ascontiguousarray(np.asarray(inputs["L0_attn_Wq"], np.float32).T)
    wk = np.ascontiguousarray(np.asarray(inputs["L0_attn_Wk"], np.float32).T)
    wv = np.ascontiguousarray(np.asarray(inputs["L0_attn_Wv"], np.float32).T)
    wo = np.ascontiguousarray(np.asarray(inputs["L0_attn_Wo"], np.float32).T)
    nw = np.ascontiguousarray(np.asarray(inputs["L0_attn_norm_w"], np.float32).reshape(128, 1))
    w1 = np.ascontiguousarray(np.asarray(inputs["L0_ffn_w1"], np.float32).T)
    w2 = np.ascontiguousarray(np.asarray(inputs["L0_ffn_w2"], np.float32).T)
    cw = np.ascontiguousarray(np.asarray(inputs["L0_ffn_conv_w"], np.float32)
                              .reshape(FCH, 128, 3).transpose(1, 0, 2).reshape(128, 96))
    cb = np.ascontiguousarray(np.asarray(inputs["L0_ffn_conv_b"], np.float32)
                              .reshape(FCH, 128).T)
    maps = []
    for core in range(8):
        bidx, qidx = divmod(core, 4)
        start = qidx * 4096
        pre = np.zeros((NPRE * 128, HID), np.float32)
        npre_real = max(0, start - 128)
        if npre_real:
            pre[:npre_real] = x[bidx, :npre_real]
        main = np.zeros((NMAIN * 128, HID), np.float32)
        lo = start - 128
        hi = start + 4096 + 128
        src_lo, src_hi = max(lo, 0), min(hi, S)
        main[src_lo - lo:src_hi - lo] = x[bidx, src_lo:src_hi]
        hm = np.ones((128, 2), np.float32)
        if qidx == 0:
            hm[:, 0] = 0.0
        if qidx == 3:
            hm[:, 1] = 0.0
        maps.append({
            "xp": pre, "xm": main, "wqT": wq, "wkT": wk, "wvT": wv, "woT": wo,
            "normw": nw, "w1T": w1, "w2T": w2, "convw": cw, "convb": cb,
            "hmask": hm,
        })
    return maps


def kernel(**inputs):
    maps = _in_maps(inputs)
    nc = build_program(NPRE, NMAIN)
    res = run_bass_kernel_spmd(nc, maps, core_ids=list(range(8)))
    out = np.empty((B, S, HID), np.float32)
    for core in range(8):
        bidx, qidx = divmod(core, 4)
        out[bidx, qidx * 4096:(qidx + 1) * 4096] = res.results[core]["out"]
    return out
